# version 02 (final)
# baseline (speedup 1.0000x reference)
import collections
import contextlib
import math
from types import SimpleNamespace

import numpy as np
import ml_dtypes

import concourse.bass as bass
import concourse.mybir as mybir
from concourse.bass_utils import run_bass_kernel_spmd

F32 = mybir.dt.float32
BF16 = mybir.dt.bfloat16
AF = mybir.ActivationFunctionType
ALU = mybir.AluOpType
AX = mybir.AxisListType.X
BFNP = ml_dtypes.bfloat16

S = 2048
DM = 4096
NT = 16
NHT = 32
NHEAD = 32
HD = 128
NEXP = 4
RK = 8
ER = 32
NCH = 16
SCL = 1.0 / math.sqrt(HD)
NEG = -1.0e5

PHASE_NAMES = ["p0", "p0b", "p1", "p2", "p3", "p4", "p5", "p6", "p6b", "p7"]
UPTO = 10

SEMS = ["dW0", "dW1", "dW2", "dW3", "dXa", "dXb",
        "dSa", "dSb", "dSc", "dSd", "dSe", "dSf",
        "dYa", "dCs", "dCg", "sT", "sV", "sS"]


class Registry:
    def __init__(self, known=None):
        self.known = known
        self.rec = {}
        self.cnt = collections.defaultdict(int)

    def inc(self, semname, key=None, n=1):
        self.cnt[semname] += n
        if key is not None:
            assert key not in self.rec, f"dup key {key}"
            self.rec[key] = self.cnt[semname]
        return self.cnt[semname]

    def tgt(self, key):
        if self.known is None:
            return 0
        assert key in self.known, f"missing target {key}"
        return self.known[key]


def I(ctx, instr, semname, key=None, n=1):
    v = ctx.R.inc(semname, key, n)
    instr.then_inc(ctx.sem[semname], n)
    return v


def W(ctx, eng, semname, key):
    eng.wait_ge(ctx.sem[semname], ctx.R.tgt(key))


def IN(ctx, e, semname, key):
    I(ctx, e.engine_nop(), semname, key)


def SN(ctx, e, key):
    I(ctx, e.memzero(ctx.sb.snop[:]), "sS", key)


def a_tile(ctx, j):
    return ctx.sb.A[:, j * S:(j + 1) * S]


def wslot(ctx, k):
    return ctx.sb.wring[:, (k % 4) * 1024:((k % 4) + 1) * 1024]


def b32(ctx, c0, n):
    return ctx.sb.B[:, c0:c0 + n]


def b16(ctx, c0, n):
    return ctx.sb.B16[:, c0:c0 + n]


def rb_ap(ctx):
    return ctx.sb.B16[:, 23552:24576].bitcast(F32)




def phases_for(ctx):
    return PHASE_NAMES[:ctx.upto]


def bar_wait(ctx, eng, i):
    for whichs in ("sT", "sV", "sS"):
        W(ctx, eng, whichs, ("bar", i, whichs))


def bar_end(ctx, eng, which, i, instr=None):
    ins = eng.drain()
    I(ctx, ins, which, ("bar", i, which))



def build(known, upto=None):
    if upto is None:
        upto = UPTO
    R = Registry(known)
    nc = bass.Bass("TRN2", target_bir_lowering=False,
                   detect_race_conditions=getattr(build, "detect_races", True))

    d = SimpleNamespace()
    d.x = nc.dram_tensor("x", [S, DM], F32, kind="ExternalInput").ap()
    for nm in ("wq", "wk", "wv", "wo", "wup"):
        setattr(d, nm, nc.dram_tensor(nm, [DM, DM], F32, kind="ExternalInput").ap())
    d.gwT = nc.dram_tensor("gwT", [DM, NEXP], F32, kind="ExternalInput").ap()
    d.apw = nc.dram_tensor("apw", [DM, ER], F32, kind="ExternalInput").ap()
    d.btw = nc.dram_tensor("btw", [ER, DM], F32, kind="ExternalInput").ap()
    d.ebias = nc.dram_tensor("ebias", [128, NEXP], F32, kind="ExternalInput").ap()
    d.ident = nc.dram_tensor("ident", [128, 128], BF16, kind="ExternalInput").ap()
    d.masktri = nc.dram_tensor("masktri", [128, 128], F32, kind="ExternalInput").ap()
    d.onesr = nc.dram_tensor("onesr", [1, 128], BF16, kind="ExternalInput").ap()
    d.onesr32 = nc.dram_tensor("onesr32", [1, 128], F32, kind="ExternalInput").ap()
    d.onesc = nc.dram_tensor("onesc", [128, 1], BF16, kind="ExternalInput").ap()
    d.cvec = nc.dram_tensor("cvec", [128, 4], F32, kind="ExternalInput").ap()
    d.cc = nc.dram_tensor("cc", [128, S], F32, kind="ExternalInput").ap()
    d.ss = nc.dram_tensor("ss", [128, S], F32, kind="ExternalInput").ap()
    d.out = nc.dram_tensor("out", [S, DM], F32, kind="ExternalOutput").ap()
    dbg = getattr(build, "debug_scratch", ())
    kindof = lambda nm: "ExternalOutput" if nm in dbg else "Internal"
    d.y1 = nc.dram_tensor("y1", [S, DM], F32, kind=kindof("y1")).ap()
    d.y2 = nc.dram_tensor("y2", [S, DM], F32, kind=kindof("y2")).ap()
    d.qsp = nc.dram_tensor("qsp", [NHEAD * 128, S], F32, kind=kindof("qsp")).ap()
    d.ksp = nc.dram_tensor("ksp", [NHEAD * 128, S], F32, kind=kindof("ksp")).ap()
    d.vsp = nc.dram_tensor("vsp", [NHEAD * 128, S], BF16, kind=kindof("vsp")).ap()

    es = contextlib.ExitStack()
    sb = SimpleNamespace()
    with es:
        def S_(name, shape, dt):
            return es.enter_context(nc.sbuf_tensor("sb_" + name, shape, dt))

        sb.A = S_("A", [128, NHT * S], BF16)
        sb.B = S_("B", [128, 12288], F32)
        sb.B16 = sb.B[:].bitcast(BF16)
        sb.wring = S_("wring", [128, 4096], BF16)
        sb.sh = S_("sh", [128, 4096], F32)
        sb.sh16 = sb.sh[:].bitcast(BF16)
        sb.ident = S_("ident", [128, 128], BF16)
        sb.masktri = S_("masktri", [128, 128], F32)
        sb.onesr = S_("onesr", [1, 128], BF16)
        sb.onesr32 = S_("onesr32", [1, 128], F32)
        sb.onesc = S_("onesc", [128, 1], BF16)
        sb.ebias = S_("ebias", [128, NEXP], F32)
        sb.gw = S_("gw", [128, NHT * NEXP], BF16)
        sb.logit = S_("logit", [128, NT * NEXP], F32)
        sb.lmax = S_("lmax", [128, 1], F32)
        sb.maskb = S_("maskb", [128, 192], BF16)
        sb.cvec = S_("cvec", [128, 4], F32)
        sb.apw = S_("apw", [128, NHT * ER], BF16)
        sb.stats = S_("stats", [128, 48], F32)
        sb.mv = S_("mv", [128, 2], F32)
        sb.lnv = S_("lnv", [128, 1], F32)
        sb.rstd = S_("rstd", [128, 1], F32)
        sb.sm = S_("sm", [128, 16], F32)
        sb.sm16 = sb.sm[:].bitcast(BF16)
        sb.negB = S_("negB", [128, 2], F32)
        sb.rc = S_("rc", [1, 512], F32)
        sb.snop = S_("snop", [128, 1], F32)

        sem = {}
        for nm in SEMS:
            sem[nm] = es.enter_context(nc.semaphore(nm))
        pq = [es.enter_context(nc.psum_tensor(f"pq{i}", [128, 1024], F32))
              for i in range(4)]

        block = es.enter_context(nc.Block())
        ctx = SimpleNamespace(nc=nc, R=R, d=d, sb=sb, sem=sem, pq=pq, upto=upto)

        @block.sync
        def _(e):
            stream_sync(ctx, e)

        @block.gpsimd
        def _(e):
            stream_gpsimd(ctx, e)

        @block.tensor
        def _(e):
            stream_tensor(ctx, e)

        @block.vector
        def _(e):
            stream_vector(ctx, e)

        @block.scalar
        def _(e):
            stream_scalar(ctx, e)

    return nc, R



def stream_sync(ctx, e):
    d, sb = ctx.d, ctx.sb
    for name in ("ident", "masktri", "onesr", "onesr32", "onesc", "ebias",
                 "cvec"):
        ins = e.dma_start(getattr(sb, name)[:], getattr(d, name)[:])
        ins.then_inc(ctx.sem["dCs"], 16)
        ctx.R.cnt["dCs"] += 16
    ctx.R.rec[("consts",)] = ctx.R.cnt["dCs"]

    names = phases_for(ctx)
    for i, ph in enumerate(names):
        if i > 0:
            bar_wait(ctx, e, i - 1)
        if ph == "p0":
            sync_ln(ctx, e, 0, d.x)
        elif ph == "p1":
            sync_resid(ctx, e, 1, d.x)
        elif ph == "p2":
            W(ctx, e, "dYa", ("p1.yst", NCH - 1))
            sync_ln(ctx, e, 2, d.y1)
        elif ph == "p4":
            sync_p4(ctx, e)
        elif ph == "p5":
            sync_resid(ctx, e, 5, d.y1)
        elif ph == "p6":
            W(ctx, e, "dYa", ("p5.yst", NCH - 1))
            sync_ln(ctx, e, 6, d.y2)
        elif ph == "p7":
            sync_resid(ctx, e, 7, d.y2)
    last_i = len(names) - 1
    bar_wait(ctx, e, last_i)
    if getattr(build, "dump_A", False):
        for j in range(NHT):
            ins = e.dma_start(ctx.d.vsp[j * 128:(j + 1) * 128, :],
                              a_tile(ctx, j))
            ins.then_inc(ctx.sem["dXa"], 16)
            ctx.R.cnt["dXa"] += 16
        e.wait_ge(ctx.sem["dXa"], ctx.R.cnt["dXa"])
    if names[-1] in ("p1", "p5", "p7"):
        W(ctx, e, "dYa", (names[-1] + ".yst", NCH - 1))
    if "p3" in names and "p4" not in names:
        for sn, key in (("dSa", ("p3.spq", 30)), ("dSb", ("p3.spq", 31)),
                        ("dSc", ("p3.spk", 30)), ("dSd", ("p3.spk", 31)),
                        ("dSe", ("p3.spv", 30)), ("dSf", ("p3.spv", 31))):
            W(ctx, e, sn, key)


def sync_ln(ctx, e, ph, src):
    sb = ctx.sb
    for r in range(NT):
        slot = r % 2
        semn = "dXa" if slot == 0 else "dXb"
        if r >= 2:
            W(ctx, e, "sV", (f"p{ph}.tpev", r - 2, 0))
        ins = e.dma_start(sb.B[:, slot * 4096:(slot + 1) * 4096],
                          src[r * 128:(r + 1) * 128, :])
        I(ctx, ins, semn, (f"p{ph}.xld", r), 16)


def sync_resid(ctx, e, ph, src):
    sb = ctx.sb
    srcv = src.rearrange("(t p) m -> p t m", p=128)
    for c in range(NCH):
        slot = c % 2
        semn = "dXa" if slot == 0 else "dXb"
        if c >= 2:
            W(ctx, e, "sV", (f"p{ph}.ev", c - 2, 3))
        for hh in range(2):
            dst = sb.B[:, slot * 4096 + hh * 2048:slot * 4096 + (hh + 1) * 2048]
            dst = dst.rearrange("p (t c) -> p t c", c=256)
            ins = e.dma_start(dst, srcv[:, hh * 8:(hh + 1) * 8,
                                        c * 256:(c + 1) * 256])
            ctx.R.cnt[semn] += 16
            ins.then_inc(ctx.sem[semn], 16)
        ctx.R.rec[(f"p{ph}.rld", c)] = ctx.R.cnt[semn]


def sync_p4(ctx, e):
    d = ctx.d
    for sn, key in (("dSa", ("p3.spq", 30)), ("dSb", ("p3.spq", 31)),
                    ("dSc", ("p3.spk", 30)), ("dSd", ("p3.spk", 31)),
                    ("dSe", ("p3.spv", 30)), ("dSf", ("p3.spv", 31))):
        W(ctx, e, sn, key)
    ins = e.dma_start(ctx.sb.sh[:, 0:2048], d.cc[:])
    ins.then_inc(ctx.sem["dCs"], 16)
    ctx.R.cnt["dCs"] += 16
    ins = e.dma_start(ctx.sb.sh[:, 2048:4096], d.ss[:])
    ins.then_inc(ctx.sem["dCs"], 16)
    ctx.R.cnt["dCs"] += 16
    ctx.R.rec[("p4.ccss",)] = ctx.R.cnt["dCs"]
    for h in range(NHEAD):
        if h >= 1:
            W(ctx, e, "sT", ("p4.vtp", h - 1))
        ins = e.dma_start(b16(ctx, 14336, 2048), d.vsp[h * 128:(h + 1) * 128, :])
        I(ctx, ins, "dXb", ("p4.vld", h), 16)
        for u in range(8):
            mat = d.qsp if u < 4 else d.ksp
            tc = u % 4
            slot = u % 2
            pu = (h, u - 2) if u >= 2 else ((h - 1, u + 6) if h >= 1 else None)
            if pu is not None:
                W(ctx, e, "sV", ("p4.rope", pu[0], pu[1]))
            r0, r1 = h * 128, (h + 1) * 128
            c0, c1 = tc * 512, (tc + 1) * 512
            ins = e.dma_start(b32(ctx, slot * 512, 512), mat[r0:r1, c0:c1])
            ins.then_inc(ctx.sem["dXa"], 16)
            ctx.R.cnt["dXa"] += 16
            ins = e.dma_start(b32(ctx, 1024 + slot * 512, 512)[0:64, :],
                              mat[r0 + 64:r1, c0:c1])
            ins.then_inc(ctx.sem["dXa"], 16)
            ctx.R.cnt["dXa"] += 16
            ins = e.dma_start(b32(ctx, 1024 + slot * 512, 512)[64:128, :],
                              mat[r0:r0 + 64, c0:c1])
            ins.then_inc(ctx.sem["dXa"], 16)
            ctx.R.cnt["dXa"] += 16
            ctx.R.rec[("p4.ld", h, u)] = ctx.R.cnt["dXa"]



def wgroup_src(wsrc, row0, ntiles, c0, ncols):
    v = wsrc[row0:row0 + ntiles * 128, c0:c0 + ncols]
    return v.rearrange("(t p) c -> p t c", p=128)


def stream_gpsimd(ctx, e):
    d, sb = ctx.d, ctx.sb
    for j in range(4):
        src = d.gwT.rearrange("(t p) c -> p t c", p=128)[:, j * 8:(j + 1) * 8, :]
        dst = sb.gw[:, j * 32:(j + 1) * 32].rearrange("p (t c) -> p t c", c=NEXP)
        ins = e.dma_start(dst, src)
        ins.then_inc(ctx.sem["dCg"], 16)
        ctx.R.cnt["dCg"] += 16
    for j in range(4):
        src = d.apw.rearrange("(t p) c -> p t c", p=128)[:, j * 8:(j + 1) * 8, :]
        dst = sb.apw[:, j * 256:(j + 1) * 256].rearrange("p (t c) -> p t c", c=ER)
        ins = e.dma_start(dst, src)
        ins.then_inc(ctx.sem["dCg"], 16)
        ctx.R.cnt["dCg"] += 16
    ins = e.dma_start(sb.sh16[0:32, 0:4096], d.btw[:, :])
    ins.then_inc(ctx.sem["dCg"], 16)
    ctx.R.cnt["dCg"] += 16
    ctx.R.rec[("gconsts",)] = ctx.R.cnt["dCg"]

    names = phases_for(ctx)
    for i, ph in enumerate(names):
        if i > 0:
            bar_wait(ctx, e, i - 1)
        if ph == "p1":
            gp_ffn(ctx, e, 1, d.wup, d.y1)
        elif ph == "p3":
            gp_p3(ctx, e)
        elif ph == "p5":
            gp_ffn(ctx, e, 5, d.wo, d.y2)
        elif ph == "p6b":
            ins = e.dma_start(sb.sh16[0:32, 0:4096], d.btw[:, :])
            ins.then_inc(ctx.sem["dCg"], 16)
            ctx.R.cnt["dCg"] += 16
            ctx.R.rec[("p6b.bt",)] = ctx.R.cnt["dCg"]
        elif ph == "p7":
            gp_ffn(ctx, e, 7, d.wup, d.out)


def gp_ffn(ctx, e, ph, wsrc, ydst):
    sb = ctx.sb
    ydstv = ydst.rearrange("(t p) m -> p t m", p=128)
    for c in range(NCH):
        for g in range(8):
            idx = c * 8 + g
            semn = f"dW{idx % 4}"
            if idx >= 4:
                W(ctx, e, "sT", (f"p{ph}.wfree", idx - 4))
            src = wgroup_src(wsrc, g * 512, 4, c * 256, 256)
            dst = wslot(ctx, idx).rearrange("p (t c) -> p t c", c=256)
            ins = e.dma_start(dst, src)
            I(ctx, ins, semn, (f"p{ph}.wld", idx), 16)
        W(ctx, e, "sV", (f"p{ph}.ev", c, 3))
        for hh in range(2):
            src = sb.B[:, 8192 + hh * 2048:8192 + (hh + 1) * 2048]
            src = src.rearrange("p (t c) -> p t c", c=256)
            ins = e.dma_start(ydstv[:, hh * 8:(hh + 1) * 8, c * 256:(c + 1) * 256],
                              src)
            ctx.R.cnt["dYa"] += 16
            ins.then_inc(ctx.sem["dYa"], 16)
        ctx.R.rec[(f"p{ph}.yst", c)] = ctx.R.cnt["dYa"]


def gp_p3(ctx, e):
    d = ctx.d
    widx = 0
    for h in range(NHEAD):
        for mi, wsrc in enumerate((d.wq, d.wk, d.wv)):
            mt = h * 3 + mi
            for g in range(4):
                semn = f"dW{widx % 4}"
                if widx >= 4:
                    W(ctx, e, "sT", ("p3.wfree", widx - 4))
                src = wgroup_src(wsrc, g * 1024, 8, h * 128, 128)
                dst = wslot(ctx, widx).rearrange("p (t c) -> p t c", c=128)
                ins = e.dma_start(dst, src)
                I(ctx, ins, semn, ("p3.wld", widx), 16)
                widx += 1
            W(ctx, e, "sS", ("p3.ev", mt))
            if mi == 0:
                ins = e.dma_start(d.qsp[h * 128:(h + 1) * 128, :], b32(ctx, 0, 2048))
                I(ctx, ins, "dSa" if h % 2 == 0 else "dSb", ("p3.spq", h), 16)
            elif mi == 1:
                ins = e.dma_start(d.ksp[h * 128:(h + 1) * 128, :], b32(ctx, 2048, 2048))
                I(ctx, ins, "dSc" if h % 2 == 0 else "dSd", ("p3.spk", h), 16)
            else:
                ins = e.dma_start(d.vsp[h * 128:(h + 1) * 128, :],
                                  b16(ctx, 16384 + (h % 2) * 2048, 2048))
                I(ctx, ins, "dSe" if h % 2 == 0 else "dSf", ("p3.spv", h), 16)



def stream_tensor(ctx, e):
    names = phases_for(ctx)
    for i, ph in enumerate(names):
        if i > 0:
            bar_wait(ctx, e, i - 1)
        last = None
        if ph == "p0":
            last = t_ln(ctx, e, 0)
        elif ph == "p0b":
            last = t_gate(ctx, e, 0)
        elif ph == "p1":
            last = t_ffn(ctx, e, 1, delta=True)
        elif ph == "p2":
            last = t_ln(ctx, e, 2)
        elif ph == "p3":
            last = t_p3(ctx, e)
        elif ph == "p4":
            last = t_p4(ctx, e)
        elif ph == "p5":
            last = t_ffn(ctx, e, 5, delta=False)
        elif ph == "p6":
            last = t_ln(ctx, e, 6)
        elif ph == "p6b":
            last = t_gate(ctx, e, 6)
        elif ph == "p7":
            last = t_ffn(ctx, e, 7, delta=True)
        bar_end(ctx, e, "sT", i)


def t_ln(ctx, e, ph):
    sb = ctx.sb
    last = None
    if ph == 0:
        W(ctx, e, "dCs", ("consts",))
    for r in range(NT):
        slot = r % 2
        xn = sb.B16[:, 16384 + slot * 4096:16384 + (slot + 1) * 4096]
        W(ctx, e, "sV", (f"p{ph}.norm", r))
        for g in range(4):
            pqg = ctx.pq[g]
            if r >= 1:
                W(ctx, e, "sV", (f"p{ph}.tpev", r - 1, g))
            for j in range(8):
                ht = g * 8 + j
                last = e.matmul(pqg[:, j * 128:(j + 1) * 128],
                                xn[:, ht * 128:(ht + 1) * 128],
                                sb.ident[:], start=True, stop=True)
            I(ctx, last, "sT", (f"p{ph}.tpg", r, g))
        ctx.R.rec[(f"p{ph}.tp", r)] = ctx.R.cnt["sT"]
    return last


def t_gate(ctx, e, ph):
    sb, pq = ctx.sb, ctx.pq
    if getattr(build, "stub_gate", False):
        return None
    W(ctx, e, "dCg", ("gconsts",))
    if ph == 6:
        W(ctx, e, "dCg", ("p6b.bt",))
    last = None
    for t in range(NT):
        if t >= 4:
            W(ctx, e, "sV", (f"p{ph}b.lev", t - 4))
        for h in range(NHT):
            last = e.matmul(pq[t % 4][:, 0:4],
                            a_tile(ctx, h)[:, t * 128:(t + 1) * 128],
                            sb.gw[:, h * 4:(h + 1) * 4],
                            start=(h == 0), stop=(h == NHT - 1))
        I(ctx, last, "sT", (f"p{ph}b.lmm", t))
    if getattr(build, "gate_cut", 5) < 2:
        return last
    W(ctx, e, "sV", (f"p{ph}b.maskall",))
    W(ctx, e, "sV", (f"p{ph}b.mb0",))
    for t in range(NT):
        pqt = pq[0] if t < 8 else pq[1]
        last = e.matmul(pqt[:, (t % 8) * 128:(t % 8 + 1) * 128],
                        sb.maskb[:, t * 4:t * 4 + 128], sb.ident[:],
                        start=True, stop=True)
        if t == 7 or t == 15:
            I(ctx, last, "sT", (f"p{ph}b.mtp", 0 if t == 7 else 1))
    if getattr(build, "gate_cut", 5) < 3:
        return last
    for t in range(NT):
        for h in range(NHT):
            last = e.matmul(pq[2][:, t * 32:(t + 1) * 32],
                            a_tile(ctx, h)[:, t * 128:(t + 1) * 128],
                            sb.apw[:, h * ER:(h + 1) * ER],
                            start=(h == 0), stop=(h == NHT - 1))
        I(ctx, last, "sT", (f"p{ph}b.zmm", t))
    if getattr(build, "gate_cut", 5) < 4:
        return last
    W(ctx, e, "sV", (f"p{ph}b.zev",))
    for t in range(NT):
        pqt = pq[0] if t < 8 else pq[1]
        last = e.matmul(pqt[:, (t % 8) * 128:(t % 8 + 1) * 128],
                        b16(ctx, 2048, 640)[:, t * 32:t * 32 + 128], sb.ident[:],
                        start=True, stop=True)
        if t == 7 or t == 15:
            I(ctx, last, "sT", (f"p{ph}b.ztp", 0 if t == 7 else 1))
    return last


def t_ffn(ctx, e, ph, delta):
    sb, pq = ctx.sb, ctx.pq
    last = None
    for c in range(NCH):
        if c >= 1:
            for q in range(4):
                W(ctx, e, "sV", (f"p{ph}.ev", c - 1, q))
        for g in range(8):
            idx = c * 8 + g
            W(ctx, e, f"dW{idx % 4}", (f"p{ph}.wld", idx))
            for k in range(4):
                h = g * 4 + k
                wt = wslot(ctx, idx)[:, k * 256:(k + 1) * 256]
                for q in range(4):
                    for tl in range(4):
                        t = q * 4 + tl
                        ins = e.matmul(
                            pq[q][:, tl * 256:(tl + 1) * 256],
                            a_tile(ctx, h)[:, t * 128:(t + 1) * 128], wt,
                            start=(h == 0 and tl % 2 == 0),
                            stop=(not delta and h == NHT - 1 and tl % 2 == 1))
                        if not delta and h == NHT - 1 and tl == 3:
                            I(ctx, ins, "sT", (f"p{ph}.mm", c, q))
                        last = ins
            if not (not delta and g == 7):
                I(ctx, last, "sT", (f"p{ph}.wfree", idx))
            else:
                ctx.R.rec[(f"p{ph}.wfree", idx)] = ctx.R.cnt["sT"]
        if delta:
            for q in range(4):
                for tl in range(4):
                    t = q * 4 + tl
                    last = e.matmul(
                        pq[q][:, tl * 256:(tl + 1) * 256],
                        sb.sh16[0:32, 4096 + t * 128:4096 + (t + 1) * 128],
                        sb.sh16[0:32, c * 256:(c + 1) * 256],
                        start=False, stop=(tl % 2 == 1))
                I(ctx, last, "sT", (f"p{ph}.mm", c, q))
    return last


def t_p3(ctx, e):
    sb, pq = ctx.sb, ctx.pq
    last = None
    widx = 0
    for h in range(NHEAD):
        for mi in range(3):
            mt = h * 3 + mi
            par = mt % 2
            if mt >= 2:
                W(ctx, e, "sS", ("p3.ev", mt - 2))
            for g in range(4):
                W(ctx, e, f"dW{widx % 4}", ("p3.wld", widx))
                for k in range(8):
                    kh = g * 8 + k
                    wt = wslot(ctx, widx)[:, k * 128:(k + 1) * 128]
                    for tc in range(4):
                        pqt = pq[2 * par + tc // 2]
                        last = e.matmul(
                            pqt[:, (tc % 2) * 512:(tc % 2 + 1) * 512], wt,
                            a_tile(ctx, kh)[:, tc * 512:(tc + 1) * 512],
                            start=(kh == 0), stop=(kh == NHT - 1))
                I(ctx, last, "sT", ("p3.wfree", widx))
                widx += 1
            ctx.R.rec[("p3.mm", mt)] = ctx.R.cnt["sT"]
    return last


def t_p4(ctx, e):
    sb, pq = ctx.sb, ctx.pq
    ctx.p4_slot_user = {}
    last = None
    for h in range(NHEAD):
        par = h % 2
        Vbuf = b16(ctx, 16384 + par * 2048, 2048)
        QTr = b16(ctx, 6144 + par * 2048, 2048)
        KTr = b16(ctx, 10240 + par * 2048, 2048)
        SQ = b16(ctx, 21504, 2048)
        W(ctx, e, "dXb", ("p4.vld", h))
        if h >= 1:
            W(ctx, e, "sV", ("p4.mqk", h - 1, 1))
        for g in range(4):
            if h >= 1 or g >= 2:
                key = ("p4.vev", h, g - 2) if g >= 2 else ("p4.vev", h - 1, g + 2)
                W(ctx, e, "sV", key)
            for j in range(4):
                kt = g * 4 + j
                last = e.matmul(
                    pq[3][:, (g % 2) * 512 + j * 128:(g % 2) * 512 + (j + 1) * 128],
                    b16(ctx, 14336, 2048)[:, kt * 128:(kt + 1) * 128],
                    sb.ident[:], start=True, stop=True)
            I(ctx, last, "sT", ("p4.vtg", h, g))
        ctx.R.rec[("p4.vtp", h)] = ctx.R.cnt["sT"]
        W(ctx, e, "sV", ("p4.vev", h, 3))
        for qk in range(2):
            W(ctx, e, "sV", ("p4.sq", h, qk))
            for cpart in range(4):
                if cpart >= 2:
                    W(ctx, e, "sV", ("p4.nrm", h, qk, cpart - 1))
                last = e.matmul(
                    pq[3][0:1, (cpart % 2) * 512:(cpart % 2) * 512 + 512],
                    sb.onesc[:], SQ[:, cpart * 512:(cpart + 1) * 512],
                    start=True, stop=True)
                I(ctx, last, "sT", ("p4.nmm", h, qk, cpart))
        W(ctx, e, "sV", ("p4.Bbf", h))
        if h >= 1:
            W(ctx, e, "sS", ("p4.rbev", h - 1, 3))
        last = e.matmul(pq[1][:, 512:513], sb.onesr[:],
                        sb.sm16[0:1, par * 16 + 12:par * 16 + 13],
                        start=True, stop=True)
        I(ctx, last, "sT", ("p4.bmm", h))
        for qc in range(4):
            nkt = 4 * qc + 4
            for kt in range(min(2, nkt)):
                _p4_score(ctx, e, h, qc, kt)
            for kt in range(nkt):
                v = kt - 4 * qc
                off = 128 * v if v >= 0 else 0
                N = 512 - off
                ej = kt % 2
                W(ctx, e, "sS", ("p4.exp", h, qc, kt))
                if kt == 0:
                    prev = (h, qc - 1) if qc >= 1 else ((h - 1, 3) if h >= 1 else None)
                    if prev is not None:
                        W(ctx, e, "sV", ("p4.rcp2", prev[0], prev[1]))
                    pv_prev = (h, qc - 2) if qc >= 2 else ((h - 1, qc + 2) if h >= 1 else None)
                    if pv_prev is not None:
                        W(ctx, e, "sV", ("p4.aev", pv_prev[0], pv_prev[1]))
                Ebuf = b16(ctx, 20480 + ej * 512, 512)
                last = e.matmul(pq[1][0:1, off:off + N], sb.onesc[:],
                                Ebuf[:, 0:N], start=(kt == 0), stop=(kt == nkt - 1))
                last = e.matmul(
                    pq[2][:, (qc % 2) * 512 + off:(qc % 2) * 512 + off + N],
                    Vbuf[:, kt * 128:(kt + 1) * 128], Ebuf[:, 0:N],
                    start=(kt == 0), stop=(kt == nkt - 1))
                I(ctx, last, "sT", ("p4.pv", h, qc, kt))
                if kt + 2 < nkt:
                    _p4_score(ctx, e, h, qc, kt + 2)
            W(ctx, e, "sV", ("p4.rcp", h, qc))
            if qc >= 1:
                W(ctx, e, "sS", ("p4.rbev", h, qc - 1))
            else:
                W(ctx, e, "sS", ("p4.B", h))
            last = e.matmul(pq[1][:, 512:1024], sb.onesr32[:], sb.rc[:],
                            start=True, stop=True)
            I(ctx, last, "sT", ("p4.bc", h, qc))
        ctx.R.rec[("p4.headdone", h)] = ctx.R.cnt["sT"]
    return last


def _p4_score(ctx, e, h, qc, kt):
    sb, pq = ctx.sb, ctx.pq
    par = h % 2
    QTr = b16(ctx, 6144 + par * 2048, 2048)
    KTr = b16(ctx, 10240 + par * 2048, 2048)
    v = kt - 4 * qc
    off = 128 * v if v >= 0 else 0
    N = 512 - off
    slot = kt % 2
    prevu = ctx.p4_slot_user.get((h, slot))
    if prevu is not None:
        nxt = _P4_SEQ[_P4_SEQ.index(prevu) + 1]
        W(ctx, e, "sS", ("p4.exp", h) + nxt)
    ctx.p4_slot_user[(h, slot)] = (qc, kt)
    last = e.matmul(pq[0][:, slot * 512:slot * 512 + N],
                    KTr[:, kt * 128:(kt + 1) * 128],
                    QTr[:, qc * 512 + off:qc * 512 + off + N],
                    start=True, stop=True)
    I(ctx, last, "sT", ("p4.sc", h, qc, kt))



def stream_vector(ctx, e):
    names = phases_for(ctx)
    for i, ph in enumerate(names):
        if i > 0:
            bar_wait(ctx, e, i - 1)
        last = None
        if ph == "p0":
            last = v_ln(ctx, e, 0)
        elif ph == "p0b":
            last = v_gate(ctx, e, 0)
        elif ph == "p1":
            last = v_ffn(ctx, e, 1, half=True)
        elif ph == "p2":
            last = v_ln(ctx, e, 2)
        elif ph == "p4":
            last = v_p4(ctx, e)
        elif ph == "p5":
            last = v_ffn(ctx, e, 5, half=False)
        elif ph == "p6":
            last = v_ln(ctx, e, 6)
        elif ph == "p6b":
            last = v_gate(ctx, e, 6)
        elif ph == "p7":
            last = v_ffn(ctx, e, 7, half=True)
        bar_end(ctx, e, "sV", i)


def v_ln(ctx, e, ph):
    sb = ctx.sb
    last = None
    if ph == 0:
        W(ctx, e, "dCs", ("consts",))
    if ph == 2:
        W(ctx, e, "dYa", ("p1.yst", NCH - 1))
    if ph == 6:
        W(ctx, e, "dYa", ("p5.yst", NCH - 1))
    for r in range(NT):
        slot = r % 2
        semn = "dXa" if slot == 0 else "dXb"
        W(ctx, e, semn, (f"p{ph}.xld", r))
        xr = sb.B[:, slot * 4096:(slot + 1) * 4096]
        if getattr(build, "use_bn", False):
            for i in range(8):
                last = e.bn_stats(sb.stats[:, 6 * i:6 * (i + 1)],
                                  xr[:, 512 * i:512 * (i + 1)])
            I(ctx, last, "sV", (f"p{ph}.bns", r))
            W(ctx, e, "sV", (f"p{ph}.bns", r))
            last = e.bn_aggr(sb.mv[:], sb.stats[:])
            I(ctx, last, "sV", (f"p{ph}.aggr", r))
        else:
            last = e.reduce_sum(sb.stats[:, 0:1], xr, AX)
            I(ctx, last, "sV", (f"p{ph}.bns", r))
            xnsc = sb.B16[:, 16384 + slot * 4096:16384 + (slot + 1) * 4096]
            last = e.tensor_tensor(xnsc, xr, xr, ALU.mult)
            I(ctx, last, "sV", (f"p{ph}.sq1", r))
            W(ctx, e, "sV", (f"p{ph}.sq1", r))
            last = e.reduce_sum(sb.stats[:, 4:5], xnsc, AX)
            I(ctx, last, "sV", (f"p{ph}.sq", r))
            W(ctx, e, "sV", (f"p{ph}.sq", r))
            last = e.tensor_scalar(sb.mv[:, 0:1], sb.stats[:, 0:1],
                                   sb.cvec[:, 2:3], None, ALU.mult)
            I(ctx, last, "sV", (f"p{ph}.mn", r))
            W(ctx, e, "sV", (f"p{ph}.mn", r))
            last = e.tensor_tensor(sb.stats[:, 1:2], sb.mv[:, 0:1], sb.mv[:, 0:1],
                                   ALU.mult)
            I(ctx, last, "sV", (f"p{ph}.m2", r))
            last = e.tensor_scalar(sb.stats[:, 2:3], sb.stats[:, 4:5],
                                   sb.cvec[:, 2:3], None, ALU.mult)
            I(ctx, last, "sV", (f"p{ph}.v1", r))
            W(ctx, e, "sV", (f"p{ph}.v1", r))
            last = e.tensor_tensor(sb.mv[:, 1:2], sb.stats[:, 2:3], sb.stats[:, 1:2],
                                   ALU.subtract)
            I(ctx, last, "sV", (f"p{ph}.aggr", r))
        W(ctx, e, "sS", (f"p{ph}.rstd", r))
        if r >= 2:
            W(ctx, e, "sT", (f"p{ph}.tp", r - 2))
        xn = sb.B16[:, 16384 + slot * 4096:16384 + (slot + 1) * 4096]
        last = e.tensor_scalar(xn, xr, sb.mv[:, 0:1], sb.rstd[:, 0:1],
                               ALU.subtract, ALU.mult)
        I(ctx, last, "sV", (f"p{ph}.norm", r))
        for g in range(4):
            W(ctx, e, "sT", (f"p{ph}.tpg", r, g))
            dst = sb.A[:].rearrange("p (h t) -> p h t", t=S)[
                :, g * 8:(g + 1) * 8, r * 128:(r + 1) * 128]
            last = e.tensor_copy(dst, ctx.pq[g][:, 0:1024])
            IN(ctx, e, "sV", (f"p{ph}.tpev", r, g))
    return last


def v_gate(ctx, e, ph):
    sb = ctx.sb
    last = None
    if getattr(build, "stub_gate", False):
        last = e.memset(sb.sh16[0:32, 4096:6144], 0.0)
        I(ctx, last, "sV", (f"p{ph}b.zm", 0))
        return last
    for t in range(NT):
        W(ctx, e, "sT", (f"p{ph}b.lmm", t))
        lg = sb.logit[:, t * 4:(t + 1) * 4]
        last = e.tensor_tensor(lg, ctx.pq[t % 4][:, 0:4], sb.ebias[:], ALU.add)
        I(ctx, last, "sV", (f"p{ph}b.ladd", t))
        W(ctx, e, "sV", (f"p{ph}b.ladd", t))
        last = e.reduce_max(sb.lmax[:], lg, AX)
        I(ctx, last, "sV", (f"p{ph}b.lmx", t))
        W(ctx, e, "sV", (f"p{ph}b.lmx", t))
        last = e.tensor_single_scalar(sb.maskb[:, t * 4:(t + 1) * 4], lg,
                                      sb.lmax[:, 0:1], ALU.is_ge)
        IN(ctx, e, "sV", (f"p{ph}b.lev", t))
    I(ctx, e.engine_nop(), "sV", (f"p{ph}b.maskall",))
    if getattr(build, "gate_cut", 5) < 2:
        return last
    last = e.memset(sb.maskb[:, 64:192], 0.0)
    I(ctx, last, "sV", (f"p{ph}b.mb0",))
    last = e.memset(b16(ctx, 2048, 640)[:, 512:640], 0.0)
    I(ctx, last, "sV", (f"p{ph}b.zb0",))
    for half in range(2):
        W(ctx, e, "sT", (f"p{ph}b.mtp", half))
        pqt = ctx.pq[0] if half == 0 else ctx.pq[1]
        last = e.tensor_copy(b16(ctx, 0, 2048)[:, half * 1024:(half + 1) * 1024],
                             pqt[:, 0:1024])
        IN(ctx, e, "sV", (f"p{ph}b.mtev", half))
    if getattr(build, "gate_cut", 5) < 3:
        return last
    W(ctx, e, "sT", (f"p{ph}b.zmm", NT - 1))
    last = e.tensor_copy(b16(ctx, 2048, 640)[:, 0:512], ctx.pq[2][:, 0:512])
    IN(ctx, e, "sV", (f"p{ph}b.zev",))
    if getattr(build, "gate_cut", 5) < 4:
        return last
    for half in range(2):
        W(ctx, e, "sT", (f"p{ph}b.ztp", half))
        pqt = ctx.pq[0] if half == 0 else ctx.pq[1]
        last = e.tensor_copy(b16(ctx, 4096, 2048)[:, half * 1024:(half + 1) * 1024],
                             pqt[:, 0:1024])
        IN(ctx, e, "sV", (f"p{ph}b.zfev", half))
    if getattr(build, "gate_cut", 5) < 5:
        return last
    rep = [p // 8 for p in range(32)]
    last = e.stream_shuffle(sb.sh16[0:32, 6144:8192], b16(ctx, 0, 2048)[0:32, :],
                            rep)
    I(ctx, last, "sV", (f"p{ph}b.me",))
    W(ctx, e, "sV", (f"p{ph}b.me",))
    for tc in range(4):
        last = e.tensor_tensor(
            sb.sh16[0:32, 4096 + tc * 512:4096 + (tc + 1) * 512],
            b16(ctx, 4096, 2048)[0:32, tc * 512:(tc + 1) * 512],
            sb.sh16[0:32, 6144 + tc * 512:6144 + (tc + 1) * 512], ALU.mult)
        I(ctx, last, "sV", (f"p{ph}b.zm", tc))
    return last


def v_ffn(ctx, e, ph, half):
    sb = ctx.sb
    last = None
    yout = sb.B[:, 8192:12288]
    for c in range(NCH):
        slot = c % 2
        semn = "dXa" if slot == 0 else "dXb"
        W(ctx, e, semn, (f"p{ph}.rld", c))
        if c >= 1:
            W(ctx, e, "dYa", (f"p{ph}.yst", c - 1))
        resid = sb.B[:, slot * 4096:(slot + 1) * 4096]
        for q in range(4):
            W(ctx, e, "sT", (f"p{ph}.mm", c, q))
            if half:
                last = e.scalar_tensor_tensor(
                    yout[:, q * 1024:(q + 1) * 1024], ctx.pq[q][:],
                    sb.cvec[:, 1:2], resid[:, q * 1024:(q + 1) * 1024],
                    ALU.mult, ALU.add)
            else:
                last = e.tensor_tensor(
                    yout[:, q * 1024:(q + 1) * 1024], ctx.pq[q][:],
                    resid[:, q * 1024:(q + 1) * 1024], ALU.add)
            IN(ctx, e, "sV", (f"p{ph}.ev", c, q))
    return last


def v_p4(ctx, e):
    sb = ctx.sb
    last = None
    W(ctx, e, "dCs", ("p4.ccss",))
    for sn, key in (("dSc", ("p3.spk", 30)), ("dSd", ("p3.spk", 31)),
                    ("dSe", ("p3.spv", 30)), ("dSf", ("p3.spv", 31))):
        W(ctx, e, sn, key)
    cc = sb.sh[:, 0:2048]
    ss = sb.sh[:, 2048:4096]
    for h in range(NHEAD):
        par = h % 2
        QTr = b16(ctx, 6144 + par * 2048, 2048)
        KTr = b16(ctx, 10240 + par * 2048, 2048)
        SQ = b16(ctx, 21504, 2048)
        if h >= 2:
            W(ctx, e, "sT", ("p4.headdone", h - 2))
        for g in range(4):
            W(ctx, e, "sT", ("p4.vtg", h, g))
            Vbuf = b16(ctx, 16384 + par * 2048, 2048)
            last = e.tensor_copy(Vbuf[:, g * 512:(g + 1) * 512],
                                 ctx.pq[3][:, (g % 2) * 512:(g % 2) * 512 + 512])
            IN(ctx, e, "sV", ("p4.vev", h, g))
        for u in range(8):
            tc = u % 4
            slot = u % 2
            W(ctx, e, "dXa", ("p4.ld", h, u))
            raw = b32(ctx, slot * 512, 512)
            rsw = b32(ctx, 1024 + slot * 512, 512)
            t1 = b32(ctx, 2048 + slot * 512, 512)
            ccx = cc[:, tc * 512:(tc + 1) * 512]
            ssx = ss[:, tc * 512:(tc + 1) * 512]
            last = e.tensor_tensor(t1, rsw, ssx, ALU.mult)
            I(ctx, last, "sV", ("p4.r1", h, u))
            W(ctx, e, "sV", ("p4.r1", h, u))
            last = e.tensor_tensor(rsw, raw, ccx, ALU.mult)
            I(ctx, last, "sV", ("p4.r2", h, u))
            W(ctx, e, "sV", ("p4.r2", h, u))
            dstmat = QTr if u < 4 else KTr
            last = e.tensor_tensor(dstmat[:, tc * 512:(tc + 1) * 512], rsw, t1,
                                   ALU.add)
            IN(ctx, e, "sV", ("p4.rope", h, u))
        for qk in range(2):
            mat = QTr if qk == 0 else KTr
            if qk == 0:
                W(ctx, e, "sV", ("p4.rope", h, 3))
            else:
                W(ctx, e, "sV", ("p4.rope", h, 7))
                W(ctx, e, "sT", ("p4.nmm", h, 0, 3))
            last = e.tensor_tensor(SQ, mat, mat, ALU.mult)
            I(ctx, last, "sV", ("p4.sq", h, qk))
            for cpart in range(4):
                W(ctx, e, "sT", ("p4.nmm", h, qk, cpart))
                last = e.reduce_max(
                    sb.sm[0:1, par * 8 + cpart:par * 8 + cpart + 1],
                    ctx.pq[3][0:1, (cpart % 2) * 512:(cpart % 2) * 512 + 512], AX)
                I(ctx, last, "sV", ("p4.nrm", h, qk, cpart))
            W(ctx, e, "sV", ("p4.nrm", h, qk, 3))
            last = e.reduce_max(sb.sm[0:1, par * 8 + 4 + qk:par * 8 + 5 + qk],
                                sb.sm[0:1, par * 8:par * 8 + 4], AX)
            I(ctx, last, "sV", ("p4.mqk", h, qk))
        W(ctx, e, "sV", ("p4.mqk", h, 1))
        last = e.tensor_tensor(sb.sm[0:1, par * 8 + 6:par * 8 + 7],
                               sb.sm[0:1, par * 8 + 4:par * 8 + 5],
                               sb.sm[0:1, par * 8 + 5:par * 8 + 6], ALU.mult)
        I(ctx, last, "sV", ("p4.mm2", h))
        W(ctx, e, "sS", ("p4.Bf", h))
        last = e.tensor_copy(sb.sm16[0:1, par * 16 + 12:par * 16 + 13],
                             sb.sm[0:1, par * 8 + 7:par * 8 + 8])
        I(ctx, last, "sV", ("p4.Bbf", h))
        for qc in range(4):
            for kt in range(4 * qc, 4 * qc + 4):
                slot = kt % 2
                W(ctx, e, "sT", ("p4.sc", h, qc, kt))
                ps = ctx.pq[0][:, slot * 512:slot * 512 + 128]
                last = e.tensor_tensor(ps, ps, sb.masktri[:], ALU.add)
                IN(ctx, e, "sV", ("p4.msk", h, qc, kt))
            nkt = 4 * qc + 4
            W(ctx, e, "sT", ("p4.pv", h, qc, nkt - 1))
            last = e.reciprocal(sb.rc[:], ctx.pq[1][0:1, 0:512])
            I(ctx, last, "sV", ("p4.rcp", h, qc))
            IN(ctx, e, "sV", ("p4.rcp2", h, qc))
            W(ctx, e, "sS", ("p4.rbev", h, qc))
            dst = sb.A[:, h * S + qc * 512:h * S + (qc + 1) * 512]
            last = e.tensor_tensor(
                dst, ctx.pq[2][:, (qc % 2) * 512:(qc % 2 + 1) * 512],
                rb_ap(ctx), ALU.mult)
            IN(ctx, e, "sV", ("p4.aev", h, qc))
    return last



def stream_scalar(ctx, e):
    names = phases_for(ctx)
    for i, ph in enumerate(names):
        if i > 0:
            bar_wait(ctx, e, i - 1)
        last = None
        if ph == "p0":
            last = s_ln(ctx, e, 0)
        elif ph == "p0b":
            last = s_gate(ctx, e, 0)
        elif ph == "p2":
            last = s_ln(ctx, e, 2)
        elif ph == "p3":
            last = s_p3(ctx, e)
        elif ph == "p4":
            last = s_p4(ctx, e)
        elif ph == "p6":
            last = s_ln(ctx, e, 6)
        elif ph == "p6b":
            last = s_gate(ctx, e, 6)
        bar_end(ctx, e, "sS", i)


def s_ln(ctx, e, ph):
    sb = ctx.sb
    last = None
    if ph == 0:
        W(ctx, e, "dCs", ("consts",))
    for r in range(NT):
        slot = r % 2
        semn = "dXa" if slot == 0 else "dXb"
        W(ctx, e, semn, (f"p{ph}.xld", r))
        W(ctx, e, "sV", (f"p{ph}.aggr", r))
        last = e.activation(sb.lnv[:], sb.mv[:, 1:2], AF.Ln, bias=sb.cvec[:, 0:1])
        I(ctx, last, "sS", (f"p{ph}.lnv", r))
        W(ctx, e, "sS", (f"p{ph}.lnv", r))
        last = e.activation(sb.rstd[:], sb.lnv[:], AF.Exp, scale=-0.5)
        I(ctx, last, "sS", (f"p{ph}.rstd", r))
    return last


def s_gate(ctx, e, ph):
    return None


def s_p3(ctx, e):
    last = None
    for h in range(NHEAD):
        for mi in range(3):
            mt = h * 3 + mi
            par = mt % 2
            W(ctx, e, "sT", ("p3.mm", mt))
            if h >= 1 and mi == 0:
                W(ctx, e, "dSa" if (h - 1) % 2 == 0 else "dSb", ("p3.spq", h - 1))
            if h >= 1 and mi == 1:
                W(ctx, e, "dSc" if (h - 1) % 2 == 0 else "dSd", ("p3.spk", h - 1))
            if h >= 2 and mi == 2:
                W(ctx, e, "dSe" if h % 2 == 0 else "dSf", ("p3.spv", h - 2))
            if mi == 0:
                dsts = [b32(ctx, 0, 1024), b32(ctx, 1024, 1024)]
            elif mi == 1:
                dsts = [b32(ctx, 2048, 1024), b32(ctx, 3072, 1024)]
            else:
                base = 16384 + (h % 2) * 2048
                dsts = [b16(ctx, base, 1024), b16(ctx, base + 1024, 1024)]
            for half in range(2):
                last = e.copy(dsts[half], ctx.pq[2 * par + half][:])
            SN(ctx, e, ("p3.ev", mt))
    return last


def s_p4(ctx, e):
    sb = ctx.sb
    last = None
    for sn, key in (("dSe", ("p3.spv", 30)), ("dSf", ("p3.spv", 31))):
        W(ctx, e, sn, key)
    for h in range(NHEAD):
        par = h % 2
        W(ctx, e, "sV", ("p4.mm2", h))
        last = e.activation(sb.sm[0:1, par * 8 + 3:par * 8 + 4],
                            sb.sm[0:1, par * 8 + 6:par * 8 + 7], AF.Ln)
        I(ctx, last, "sS", ("p4.lnB", h))
        W(ctx, e, "sS", ("p4.lnB", h))
        last = e.activation(sb.sm[0:1, par * 8 + 7:par * 8 + 8],
                            sb.sm[0:1, par * 8 + 3:par * 8 + 4], AF.Exp, scale=0.5)
        I(ctx, last, "sS", ("p4.Bf", h))
        W(ctx, e, "sT", ("p4.bmm", h))
        last = e.activation(sb.negB[:, par:par + 1], ctx.pq[1][:, 512:513],
                            AF.Copy, scale=-SCL)
        SN(ctx, e, ("p4.B", h))
        for qc in range(4):
            nkt = 4 * qc + 4
            for kt in range(nkt):
                v = kt - 4 * qc
                diag = v >= 0
                off = 128 * v if diag else 0
                N = 512 - off
                slot = kt % 2
                ej = kt % 2
                W(ctx, e, "sT", ("p4.sc", h, qc, kt))
                if diag:
                    W(ctx, e, "sV", ("p4.msk", h, qc, kt))
                prev = _e_prev(h, qc, kt)
                if prev is not None:
                    W(ctx, e, "sT", ("p4.pv",) + prev)
                Ebuf = b16(ctx, 20480 + ej * 512, 512)
                last = e.activation(Ebuf[:, 0:N],
                                    ctx.pq[0][:, slot * 512:slot * 512 + N],
                                    AF.Exp, bias=sb.negB[:, par:par + 1],
                                    scale=SCL)
                I(ctx, last, "sS", ("p4.exp", h, qc, kt))
            W(ctx, e, "sT", ("p4.bc", h, qc))
            prevrb = (h, qc - 1) if qc >= 1 else ((h - 1, 3) if h >= 1 else None)
            if prevrb is not None:
                W(ctx, e, "sV", ("p4.aev", prevrb[0], prevrb[1]))
            last = e.copy(rb_ap(ctx), ctx.pq[1][:, 512:1024])
            SN(ctx, e, ("p4.rbev", h, qc))
    return last


_P4_SEQ = []
for _q in range(4):
    for _k in range(4 * _q + 4):
        _P4_SEQ.append((_q, _k))


def _e_prev(h, qc, kt):
    pos = _P4_SEQ.index((qc, kt))
    ej = kt % 2
    for p in range(pos - 1, -1, -1):
        if _P4_SEQ[p][1] % 2 == ej:
            return (h,) + _P4_SEQ[p]
    if h == 0:
        return None
    for p in range(len(_P4_SEQ) - 1, -1, -1):
        if _P4_SEQ[p][1] % 2 == ej:
            return (h - 1,) + _P4_SEQ[p]
    return None



def _host_tables():
    t = np.arange(S, dtype=np.float64)
    inv = 1.0 / (10000.0 ** (np.arange(0, HD, 2)[:64].astype(np.float64) / HD))
    f = np.outer(t, inv)
    cos = np.cos(f).astype(np.float32)
    sin = np.sin(f).astype(np.float32)
    cc = np.ascontiguousarray(np.concatenate([cos.T, cos.T], axis=0))
    ss = np.ascontiguousarray(np.concatenate([-sin.T, sin.T], axis=0))
    ident = np.eye(128, dtype=BFNP)
    masktri = np.where(np.arange(128)[:, None] <= np.arange(128)[None, :],
                       0.0, NEG).astype(np.float32)
    onesr = np.ones((1, 128), dtype=BFNP)
    onesr32 = np.ones((1, 128), dtype=np.float32)
    onesc = np.ones((128, 1), dtype=BFNP)
    cvec = np.zeros((128, 4), dtype=np.float32)
    cvec[:, 0] = 1e-5
    cvec[:, 1] = 0.5
    cvec[:, 2] = 1.0 / DM
    exp8 = np.zeros((NEXP, ER), dtype=BFNP)
    for ex in range(NEXP):
        exp8[ex, ex * RK:(ex + 1) * RK] = 1
    return cc, ss, ident, masktri, onesr, onesr32, onesc, cvec, exp8


_PERM = None


def _head_perm():
    global _PERM
    if _PERM is None:
        p = []
        for h in range(NHEAD):
            base = h * HD
            p += [base + 2 * i for i in range(64)]
            p += [base + 2 * i + 1 for i in range(64)]
        _PERM = np.array(p)
    return _PERM


_CACHED = {}


def get_program(upto=None):
    key = (upto or UPTO, tuple(sorted(getattr(build, "debug_scratch", ()))))
    if key not in _CACHED:
        _, R0 = build(None, upto)
        nc1, _ = build(R0.rec, upto)
        _CACHED[key] = nc1
    return _CACHED[key]


def make_inputs(L0_attn_x, L0_attn_wq, L0_attn_wk, L0_attn_wv, L0_attn_wo,
                L0_ffn_W_up, L0_ffn_gate_w, L0_ffn_expert_bias, L0_ffn_lora_A,
                L0_ffn_lora_B):
    x = np.ascontiguousarray(np.asarray(L0_attn_x, dtype=np.float32))
    perm = _head_perm()
    wq = np.ascontiguousarray(np.asarray(L0_attn_wq, np.float32)[:, perm])
    wk = np.ascontiguousarray(np.asarray(L0_attn_wk, np.float32)[:, perm])
    wv = np.ascontiguousarray(np.asarray(L0_attn_wv, np.float32))
    wo = np.ascontiguousarray(np.asarray(L0_attn_wo, np.float32))
    wup = np.ascontiguousarray(np.asarray(L0_ffn_W_up, np.float32).T)
    gwT = np.ascontiguousarray(np.asarray(L0_ffn_gate_w, np.float32).T)
    ebias = np.ascontiguousarray(
        np.broadcast_to(np.asarray(L0_ffn_expert_bias, np.float32)[None, :],
                        (128, NEXP)))
    apw = np.ascontiguousarray(
        np.asarray(L0_ffn_lora_A, np.float32).transpose(2, 0, 1).reshape(DM, ER))
    btw = np.ascontiguousarray(
        np.asarray(L0_ffn_lora_B, np.float32).transpose(0, 2, 1).reshape(ER, DM))
    cc, ss, ident, masktri, onesr, onesr32, onesc, cvec, exp8 = _host_tables()
    base = dict(wq=wq, wk=wk, wv=wv, wo=wo, wup=wup, gwT=gwT, apw=apw, btw=btw,
                ebias=ebias, ident=ident, masktri=masktri,
                onesr=onesr, onesr32=onesr32, onesc=onesc, cvec=cvec,
                cc=cc, ss=ss)
    in_maps = []
    for c in range(8):
        m = dict(base)
        m["x"] = np.ascontiguousarray(x[c])
        in_maps.append(m)
    return in_maps


def kernel(**inputs):
    in_maps = make_inputs(**inputs)
    nc = get_program()
    res = run_bass_kernel_spmd(nc, in_maps, core_ids=list(range(8)))
    out = np.stack([np.asarray(r["out"]) for r in res.results], axis=0)
    return out
